# version 01
# baseline (speedup 1.0000x reference)
import numpy as np
import ml_dtypes
from contextlib import ExitStack

import concourse.bass as bass
import concourse.mybir as mybir
from concourse.bass_utils import run_bass_kernel_spmd

F32 = mybir.dt.float32
BF16 = mybir.dt.bfloat16
AF = mybir.ActivationFunctionType
ALU = mybir.AluOpType
AX = mybir.AxisListType

NCORES = 8
B, S, E, H = 32, 2048, 1024, 4096
SEQ = B // NCORES
EC = E // 128
HC = H // 128
NQT = S // 128
GRP = 1024
NG = S // GRP
EPS = 1e-5


class Cnt:
    def __init__(self, h):
        self.h = h
        self.c = 0


class Emit:
    def __init__(self):
        self.ops = {k: [] for k in ("tensor", "vector", "scalar", "gpsimd", "sync")}

    def op(self, eng, fn, inc=None, n=1):
        if inc is not None:
            inc.c += n
            self.ops[eng].append(lambda e, fn=fn, h=inc.h, n=n: fn(e).then_inc(h, n))
            return inc.c
        self.ops[eng].append(lambda e, fn=fn: fn(e))
        return None

    def wait(self, eng, sem, val):
        if val and val > 0:
            self.ops[eng].append(lambda e, h=sem.h, v=val: e.wait_ge(h, v))


def build_program(n_sub=4, neg_gamma=False, kinds=None, dump=()):
    nc = bass.Bass("TRN2", target_bir_lowering=False)
    dump = set(dump)
    dbg_t = {}
    for tag in dump:
        w = 32768 if tag == "PU" else (2048 if tag in ("rsbc", "sqbc") else 16384)
        dbg_t[tag] = nc.dram_tensor("dbg_" + tag, [128, w], F32, kind="ExternalOutput").ap()

    xin = nc.dram_tensor("xin", [SEQ, EC, 128, S], F32, kind="ExternalInput").ap()
    y = nc.dram_tensor("y", [SEQ, EC, 128, S], F32, kind="ExternalOutput").ap()
    xres = nc.dram_tensor("xres", [SEQ, EC, 128, S], F32).ap()
    wA = nc.dram_tensor("wA", [HC, 128, 2048], F32, kind="ExternalInput").ap()
    wH1 = nc.dram_tensor("wH1", [HC, 128, 4096], F32, kind="ExternalInput").ap()
    wH2 = nc.dram_tensor("wH2", [HC, 128, 4096], F32, kind="ExternalInput").ap()
    wW3 = nc.dram_tensor("wW3", [EC, 128, 4096], F32, kind="ExternalInput").ap()
    wQK = nc.dram_tensor("wQK", [5, EC, 128, 1024], F32, kind="ExternalInput").ap()
    svec4 = nc.dram_tensor("svec4", [32, 512], F32, kind="ExternalInput").ap()
    svec3 = nc.dram_tensor("svec3", [8, 128], F32, kind="ExternalInput").ap()
    gam = nc.dram_tensor("gam", [1, 1], F32, kind="ExternalInput").ap()
    idf = nc.dram_tensor("idf", [128, 128], F32, kind="ExternalInput").ap()

    with ExitStack() as es:
        ec_ = es.enter_context
        S0 = ec_(nc.sbuf_tensor("S0", [128, 16384], BF16))
        S1 = ec_(nc.sbuf_tensor("S1", [128, 16384], BF16))
        S2 = ec_(nc.sbuf_tensor("S2", [128, 16384], BF16))
        S3 = ec_(nc.sbuf_tensor("S3", [128, 16384], BF16))
        SB = ec_(nc.sbuf_tensor("SB", [128, 32768], BF16))
        identf = ec_(nc.sbuf_tensor("identf", [128, 128], F32))
        identb = ec_(nc.sbuf_tensor("identb", [128, 128], BF16))
        s1c = ec_(nc.sbuf_tensor("s1c", [128, 32], F32))
        s2c = ec_(nc.sbuf_tensor("s2c", [128, 32], F32))
        h1c = ec_(nc.sbuf_tensor("h1c", [128, 32], F32))
        h2c = ec_(nc.sbuf_tensor("h2c", [128, 32], F32))
        w3c = ec_(nc.sbuf_tensor("w3c", [128, 8], F32))
        gneg = ec_(nc.sbuf_tensor("gneg", [128, 1], F32))
        ones128b = ec_(nc.sbuf_tensor("ones128b", [128, 1], BF16))
        onesrow = ec_(nc.sbuf_tensor("onesrow", [1, 512], F32))
        sqcol = ec_(nc.sbuf_tensor("sqcol", [128, 16], F32))
        nmcol = ec_(nc.sbuf_tensor("nmcol", [128, 4], F32))
        svst = ec_(nc.sbuf_tensor("svst", [32, 512], F32))
        svst3 = ec_(nc.sbuf_tensor("svst3", [8, 128], F32))
        g11 = ec_(nc.sbuf_tensor("g11", [1, 1], F32))
        epsc = ec_(nc.sbuf_tensor("epsc", [1, 1], F32))
        ps0 = ec_(nc.psum_tensor("ps0", [128, 2048], F32))
        ps1 = ec_(nc.psum_tensor("ps1", [128, 2048], F32))
        swdge = Cnt(ec_(nc.semaphore("swdge")))
        swuse = Cnt(ec_(nc.semaphore("swuse")))
        sx = Cnt(ec_(nc.semaphore("sx")))
        sst = Cnt(ec_(nc.semaphore("sst")))
        spe = Cnt(ec_(nc.semaphore("spe")))
        sdv = Cnt(ec_(nc.semaphore("sdv")))
        sac = Cnt(ec_(nc.semaphore("sac")))
        ALLSEMS = [swdge, swuse, sx, sst, spe, sdv, sac]
        block = ec_(nc.Block())

        em = Emit()
        S0f = S0[:].bitcast(F32)
        S1f = S1[:].bitcast(F32)
        S2f = S2[:].bitcast(F32)
        S3f = S3[:].bitcast(F32)
        SBf = SB[:].bitcast(F32)
        ps0b = ps0[:].bitcast(BF16)
        ps1b = ps1[:].bitcast(BF16)

        ONE = onesrow[0:1, 0:128]
        ONE_MEAN = onesrow[0:1, 128:256]
        ONE_E = onesrow[0:1, 256:384]
        ONE_NEG = onesrow[0:1, 384:512]

        mbc = SBf[:, 0:2048]
        rstdbc = SBf[:, 2048:4096]
        xln = [SBf[:, 4096 + i * 2048: 6144 + i * 2048] for i in range(2)]
        row_s = SBf[0:1, 8192:10240]
        row_rstd = SBf[0:1, 10240:12288]
        row_t2 = SBf[0:1, 12288:14336]
        row_v = SBf[32:33, 8192:10240]
        row_lg = SBf[32:33, 10240:12288]
        row_inv = SBf[32:33, 12288:14336]
        xsq = [SB[:, 24576 + i * 2048: 26624 + i * 2048] for i in range(2)]
        xbf = [SB[:, 28672 + i * 2048: 30720 + i * 2048] for i in range(2)]
        pv_lg = S0f[0:1, 4096:6144]
        pv_rs = S0f[0:1, 6144:8192]

        def fence(engs=("tensor", "vector", "scalar", "gpsimd", "sync")):
            for e in engs:
                for sm in ALLSEMS:
                    em.wait(e, sm, sm.c)

        def dump_slot(tag, ap_src, s):
            if tag not in dump or s != 0:
                return
            fence(("gpsimd",))
            em.op("gpsimd", lambda e, d=dbg_t[tag][:, :], sr=ap_src: e.dma_start(out=d, in_=sr), sst, 16)
            em.wait("gpsimd", sst, sst.c)

        wp_i = [0]

        def wpanel(dst_ap, src_ap):
            i = wp_i[0]
            wp_i[0] += 1
            if i >= 2:
                em.wait("gpsimd", swuse, i - 1)
            em.op("gpsimd", lambda e, d=dst_ap, sr=src_ap: e.dma_start(out=d, in_=sr), swdge, 16)
            return swdge.c

        def ln_phase(s, src, want_sq):
            fence()
            pe_ec = {}
            dv_done = {}
            ac_done = {}
            for ecu in range(EC):
                b = ecu % 2
                if ecu >= 2:
                    em.wait("sync", sdv, dv_done[ecu - 2])
                    em.wait("sync", sac, ac_done[ecu - 2])
                em.op("sync", lambda e, d=xln[b], sr=src[s, ecu]: e.dma_start(out=d, in_=sr), sx, 16)
                x_ld = sx.c
                em.wait("vector", sx, x_ld)
                if ecu >= 2:
                    em.wait("vector", spe, pe_ec[ecu - 2])
                em.op("vector", lambda e, d=xbf[b], sr=xln[b]: e.tensor_copy(out=d, in_=sr), sdv)
                dv_done[ecu] = sdv.c
                em.wait("scalar", sx, x_ld)
                if ecu >= 2:
                    em.wait("scalar", spe, pe_ec[ecu - 2])
                em.op("scalar", lambda e, d=xsq[b], sr=xln[b]: e.activation(out=d, in_=sr, func=AF.Square), sac)
                ac_done[ecu] = sac.c
                em.wait("tensor", sdv, dv_done[ecu])
                em.wait("tensor", sac, ac_done[ecu])
                for nb in range(4):
                    em.op("tensor", lambda e, o=ps1[0:1, nb * 512:(nb + 1) * 512], r=xbf[b][:, nb * 512:(nb + 1) * 512],
                          st=(ecu == 0), sp=(ecu == EC - 1):
                          e.matmul(o, ones128b[:, 0:1], r, start=st, stop=sp))
                for nb in range(4):
                    em.op("tensor", lambda e, o=ps0[0:1, nb * 512:(nb + 1) * 512], r=xsq[b][:, nb * 512:(nb + 1) * 512],
                          st=(ecu == 0), sp=(ecu == EC - 1):
                          e.matmul(o, ones128b[:, 0:1], r, start=st, stop=sp),
                          spe if nb == 3 else None)
                pe_ec[ecu] = spe.c
            stats_pe = spe.c
            em.wait("vector", spe, stats_pe)
            em.op("vector", lambda e: e.tensor_copy(out=row_s, in_=ps1[0:1, :]), sdv)
            em.op("vector", lambda e: e.tensor_scalar(out=row_v, in0=ps1[0:1, :], scalar1=1.0 / E,
                                                      scalar2=None, op0=ALU.mult), sdv)
            em.op("vector", lambda e: e.tensor_tensor(out=row_inv, in0=row_v, in1=row_v, op=ALU.mult), sdv)
            em.op("vector", lambda e: e.scalar_tensor_tensor(out=row_v, in0=ps0[0:1, :], scalar=1.0 / E,
                                                             in1=row_inv, op0=ALU.mult, op1=ALU.subtract), sdv)
            rows_dv = sdv.c
            em.wait("scalar", sdv, rows_dv)
            em.op("scalar", lambda e: e.activation(out=row_lg, in_=row_v, func=AF.Ln, bias=epsc[0:1, 0:1]), sac)
            em.op("scalar", lambda e: e.activation(out=row_rstd, in_=row_lg, func=AF.Exp, scale=-0.5), sac)
            if want_sq:
                em.op("scalar", lambda e: e.activation(out=row_inv, in_=row_lg, func=AF.Exp, scale=-1.0), sac)
                em.wait("vector", sac, sac.c)
                em.op("vector", lambda e: e.tensor_tensor(out=row_t2, in0=row_v, in1=row_inv, op=ALU.mult), sdv)
            rows_done_ac = sac.c
            rows_done_dv = sdv.c
            em.wait("tensor", sdv, rows_done_dv)
            em.wait("tensor", sac, rows_done_ac)
            mbc_pe = None
            for nb in range(4):
                mbc_pe = em.op("tensor", lambda e, o=ps0[:, nb * 512:(nb + 1) * 512],
                               r=row_s[0:1, nb * 512:(nb + 1) * 512]:
                               e.matmul(o, ONE_MEAN, r, start=True, stop=True), spe if nb == 3 else None)
            rstd_pe = None
            for nb in range(4):
                rstd_pe = em.op("tensor", lambda e, o=ps1[:, nb * 512:(nb + 1) * 512],
                                r=row_rstd[0:1, nb * 512:(nb + 1) * 512]:
                                e.matmul(o, ONE, r, start=True, stop=True), spe if nb == 3 else None)
            em.wait("scalar", spe, mbc_pe)
            mbc_ac = em.op("scalar", lambda e: e.copy(out=mbc, in_=ps0[:, :]), sac)
            em.wait("vector", spe, rstd_pe)
            rstd_dv = em.op("vector", lambda e: e.tensor_copy(out=rstdbc, in_=ps1[:, :]), sdv)
            sq_dv = None
            if want_sq:
                em.wait("tensor", sac, mbc_ac)
                sqb_pe = None
                for nb in range(4):
                    sqb_pe = em.op("tensor", lambda e, o=ps0[:, nb * 512:(nb + 1) * 512],
                                   r=row_t2[0:1, nb * 512:(nb + 1) * 512]:
                                   e.matmul(o, ONE_E, r, start=True, stop=True), spe if nb == 3 else None)
                em.wait("vector", spe, sqb_pe)
                em.op("vector", lambda e: e.tensor_copy(out=S2f[:, 0:2048], in_=ps0[:, :]), sdv)
                sq_dv = sdv.c
                em.wait("tensor", sdv, sq_dv)
                sqc_dv = {}
                for qt in range(NQT):
                    bq = qt % 2
                    if qt >= 2:
                        em.wait("tensor", sdv, sqc_dv[qt - 2])
                    tp = em.op("tensor", lambda e, o=ps0[:, bq * 512: bq * 512 + 128],
                               i=S2f[:, qt * 128:(qt + 1) * 128]:
                               e.transpose(o, i, identf[:, :]), spe)
                    em.wait("vector", spe, tp)
                    sqc_dv[qt] = em.op("vector", lambda e, o=sqcol[:, qt:qt + 1], i=ps0[:, bq * 512: bq * 512 + 1]:
                                       e.tensor_copy(out=o, in_=i), sdv)
                sq_dv = sdv.c
            em.wait("sync", sdv, rows_done_dv)
            em.wait("sync", sac, rows_done_ac)
            xn_dv = {}
            for ecu in range(EC):
                b = ecu % 2
                if ecu >= 2:
                    em.wait("sync", sdv, xn_dv[ecu - 2])
                em.op("sync", lambda e, d=xln[b], sr=src[s, ecu]: e.dma_start(out=d, in_=sr), sx, 16)
                x_ld = sx.c
                em.wait("vector", sx, x_ld)
                if ecu == 0:
                    em.wait("vector", sac, mbc_ac)
                    em.wait("vector", sdv, rstd_dv)
                em.op("vector", lambda e, d=xln[b]: e.tensor_tensor(out=d, in0=d, in1=mbc, op=ALU.subtract), sdv)
                em.op("vector", lambda e, d=S0[:, ecu * S:(ecu + 1) * S], sr=xln[b]:
                      e.tensor_tensor(out=d, in0=sr, in1=rstdbc, op=ALU.mult), sdv)
                xn_dv[ecu] = sdv.c
            return sdv.c, sq_dv

        def emit_pv(valT, symmetric, writer, rsbc_ap):
            if symmetric:
                srow_pe = None
                for nb in range(4):
                    for jt in range(NQT):
                        srow_pe = em.op("tensor", lambda e, o=ps0[0:1, nb * 512:(nb + 1) * 512],
                                        r=SB[:, jt * S + nb * 512: jt * S + (nb + 1) * 512],
                                        st=(jt == 0), sp=(jt == NQT - 1):
                                        e.matmul(o, ones128b[:, 0:1], r, start=st, stop=sp),
                                        spe if (jt == NQT - 1 and nb == 3) else None)
                em.wait("scalar", spe, srow_pe)
                em.op("scalar", lambda e: e.activation(out=pv_lg, in_=ps0[0:1, :], func=AF.Ln), sac)
                rs_ac = em.op("scalar", lambda e: e.activation(out=pv_rs, in_=pv_lg, func=AF.Exp, scale=-1.0), sac)
                em.wait("tensor", sac, rs_ac)
                rsb_pe = None
                for nb in range(4):
                    rsb_pe = em.op("tensor", lambda e, o=ps1[:, nb * 512:(nb + 1) * 512],
                                   r=pv_rs[0:1, nb * 512:(nb + 1) * 512]:
                                   e.matmul(o, ONE, r, start=True, stop=True), spe if nb == 3 else None)
                em.wait("vector", spe, rsb_pe)
                rsbc_dv = em.op("vector", lambda e, o=rsbc_ap: e.tensor_copy(out=o, in_=ps1[:, :]), sdv)
                em.wait("tensor", sdv, rsbc_dv)
                qrot = [ps0[:, i * 512:(i + 1) * 512] for i in range(4)] + \
                       [ps1[:, i * 512:(i + 1) * 512] for i in range(4)]
                ev_cnt = {}
                for m in range(EC):
                    for ib in range(4):
                        qi = (m * 4 + ib) % 8
                        if m * 4 + ib >= 8:
                            em.wait("tensor", sdv, ev_cnt[m * 4 + ib - 8])
                        pe_cnt = None
                        for jt in range(NQT):
                            pe_cnt = em.op("tensor", lambda e, o=qrot[qi],
                                           l=valT[:, jt * 1024 + m * 128: jt * 1024 + (m + 1) * 128],
                                           r=SB[:, jt * S + ib * 512: jt * S + (ib + 1) * 512],
                                           st=(jt == 0), sp=(jt == NQT - 1):
                                           e.matmul(o, l, r, start=st, stop=sp),
                                           spe if jt == NQT - 1 else None)
                        ev_cnt[m * 4 + ib] = writer(m, ib, qrot[qi], pe_cnt, rsbc_ap)
            else:
                PUT = S0
                qrot = [ps1[:, i * 512:(i + 1) * 512] for i in range(4)]
                ev_cnt = {}
                tp_dv = {}
                for ib in range(4):
                    for jt in range(NQT):
                        bb = jt % 2
                        if (ib, jt) >= (0, 2):
                            key = (ib, jt - 2) if jt >= 2 else (ib - 1, NQT - 2 + jt)
                            em.wait("tensor", sdv, tp_dv.get(key, 0))
                        tp_pe = None
                        for st_ in range(4):
                            qt_src = ib * 4 + st_
                            tp_pe = em.op("tensor",
                                          lambda e, o=ps0b[:, bb * 1024 + st_ * 128: bb * 1024 + (st_ + 1) * 128],
                                          i=SB[:, qt_src * S + jt * 128: qt_src * S + (jt + 1) * 128]:
                                          e.transpose(o, i, identb[:, :]), spe if st_ == 3 else None)
                        em.wait("vector", spe, tp_pe)
                        tp_dv[(ib, jt)] = em.op("vector", lambda e, o=PUT[:, jt * 512:(jt + 1) * 512],
                                                i=ps0b[:, bb * 1024: bb * 1024 + 512]:
                                                e.tensor_copy(out=o, in_=i), sdv)
                    put_dv = sdv.c
                    em.wait("tensor", sdv, put_dv)
                    s_pe = None
                    for jt in range(NQT):
                        s_pe = em.op("tensor", lambda e, o=ps0[0:1, 1024:1536], r=PUT[:, jt * 512:(jt + 1) * 512],
                                     st=(jt == 0), sp=(jt == NQT - 1):
                                     e.matmul(o, ones128b[:, 0:1], r, start=st, stop=sp),
                                     spe if jt == NQT - 1 else None)
                    em.wait("scalar", spe, s_pe)
                    em.op("scalar", lambda e: e.activation(out=pv_lg[0:1, 0:512], in_=ps0[0:1, 1024:1536], func=AF.Ln), sac)
                    rs_ac = em.op("scalar", lambda e: e.activation(out=pv_rs[0:1, 0:512], in_=pv_lg[0:1, 0:512],
                                                                   func=AF.Exp, scale=-1.0), sac)
                    em.wait("tensor", sac, rs_ac)
                    rsb_pe = em.op("tensor", lambda e: e.matmul(ps0[:, 1536:2048], ONE, pv_rs[0:1, 0:512],
                                                                start=True, stop=True), spe)
                    em.wait("vector", spe, rsb_pe)
                    rsbc_dv = em.op("vector", lambda e, o=rsbc_ap[:, ib * 512:(ib + 1) * 512]:
                                    e.tensor_copy(out=o, in_=ps0[:, 1536:2048]), sdv)
                    em.wait("tensor", sdv, rsbc_dv)
                    for m in range(EC):
                        qi = m % 4
                        if ib * EC + m >= 4:
                            em.wait("tensor", sdv, ev_cnt[ib * EC + m - 4])
                        pe_cnt = None
                        for jt in range(NQT):
                            pe_cnt = em.op("tensor", lambda e, o=qrot[qi],
                                           l=valT[:, jt * 1024 + m * 128: jt * 1024 + (m + 1) * 128],
                                           r=PUT[:, jt * 512:(jt + 1) * 512],
                                           st=(jt == 0), sp=(jt == NQT - 1):
                                           e.matmul(o, l, r, start=st, stop=sp),
                                           spe if jt == NQT - 1 else None)
                        ev_cnt[ib * EC + m] = writer(m, ib, qrot[qi], pe_cnt, rsbc_ap)

        def attn_a_phase(s, src, dst, xn_dv_c, sq_dv_c):
            XN, XNT = S0, S1
            symmetric = not neg_gamma
            sqbc = S2f[:, 0:2048]
            SM0 = S2f[:, 2048:4096]
            SM1 = S2f[:, 4096:6144]
            rsbc = S2f[:, 6144:8192]
            xst = [S3f[:, i * 2048:(i + 1) * 2048] for i in range(2)]
            yst = [S3f[:, 4096 + i * 2048: 6144 + i * 2048] for i in range(2)]
            em.wait("tensor", sdv, xn_dv_c)
            tr_dv = {}
            for jt in range(NQT):
                bb = jt % 2
                if jt >= 2:
                    em.wait("tensor", sdv, tr_dv[jt - 2])
                tp_pe = None
                for ecu in range(EC):
                    tp_pe = em.op("tensor", lambda e, o=ps1b[:, bb * 1024 + ecu * 128: bb * 1024 + (ecu + 1) * 128],
                                  i=XN[:, ecu * S + jt * 128: ecu * S + (jt + 1) * 128]:
                                  e.transpose(o, i, identb[:, :]), spe if ecu == EC - 1 else None)
                em.wait("vector", spe, tp_pe)
                tr_dv[jt] = em.op("vector", lambda e, o=XNT[:, jt * 1024:(jt + 1) * 1024],
                                  i=ps1b[:, bb * 1024: bb * 1024 + 1024]:
                                  e.tensor_copy(out=o, in_=i), sdv)
            ev1 = {}
            ex_ac = {}
            for qt in range(NQT):
                psQ = ps0 if qt % 2 == 0 else ps1
                if qt == 0:
                    em.wait("tensor", sdv, sq_dv_c)
                if qt == 1:
                    em.wait("tensor", sdv, tr_dv[NQT - 1])
                if qt >= 2:
                    em.wait("tensor", sdv, ev1[qt - 2])
                xy_pe = None
                for ecu in range(EC):
                    for nb in range(4):
                        xy_pe = em.op("tensor", lambda e, o=psQ[:, nb * 512:(nb + 1) * 512],
                                      l=XN[:, ecu * S + qt * 128: ecu * S + qt * 128 + 128],
                                      r=XN[:, ecu * S + nb * 512: ecu * S + (nb + 1) * 512],
                                      st=(ecu == 0), sp=(ecu == EC - 1):
                                      e.matmul(o, l, r, start=st, stop=sp),
                                      spe if (ecu == EC - 1 and nb == 3) else None)
                em.wait("vector", spe, xy_pe)
                if qt >= 1:
                    em.wait("vector", sac, ex_ac[qt - 1])
                ev1[qt] = em.op("vector", lambda e, i=psQ[:, :], sc=sqcol[:, qt:qt + 1]:
                                e.tensor_scalar(out=SM0, in0=i, scalar1=-2.0, scalar2=sc,
                                                op0=ALU.mult, op1=ALU.add), sdv)
                em.op("vector", lambda e: e.tensor_tensor(out=SM1, in0=SM0, in1=sqbc, op=ALU.add), sdv)
                l_dv = em.op("vector", lambda e: e.tensor_scalar(out=SM0, in0=SM1, scalar1=0.0,
                                                                 scalar2=gneg[:, 0:1], op0=ALU.max, op1=ALU.mult), sdv)
                if symmetric:
                    em.wait("scalar", sdv, l_dv)
                    ex_ac[qt] = em.op("scalar", lambda e, o=SB[:, qt * S:(qt + 1) * S]:
                          e.activation(out=o, in_=SM0, func=AF.Exp), sac)
                else:
                    nm_dv = em.op("vector", lambda e, o=nmcol[:, qt % 4: qt % 4 + 1]:
                                  e.reduce_max(out=o, in_=SM0, axis=AX.X, negate=True), sdv)
                    em.wait("scalar", sdv, nm_dv)
                    ex_ac[qt] = em.op("scalar", lambda e, o=SB[:, qt * S:(qt + 1) * S], b_=nmcol[:, qt % 4: qt % 4 + 1]:
                          e.activation(out=o, in_=SM0, func=AF.Exp, bias=b_), sac)
            em.wait("tensor", sac, sac.c)
            em.wait("tensor", sdv, ev1[NQT - 1])

            dump_slot("xn", S0[:, :], s)
            dump_slot("PU", SB[:, :], s)
            dump_slot("sqbc", sqbc, s)
            wr_dv = {}
            y_store = {}

            def writer(m, ib, ps_ap, pe_cnt, rsbc_ap):
                if ib == 0:
                    if m >= 2:
                        em.wait("sync", sdv, wr_dv.get((m - 2, 3), 0))
                    em.op("sync", lambda e, d=xst[m % 2], sr=src[s, m]: e.dma_start(out=d, in_=sr), sx, 16)
                    wr_dv[("xld", m)] = sx.c
                em.wait("vector", spe, pe_cnt)
                if ib == 0:
                    em.wait("vector", sx, wr_dv[("xld", m)])
                    em.wait("vector", sst, y_store.get(m - 2, 0))
                em.op("vector", lambda e, o=yst[m % 2][:, ib * 512:(ib + 1) * 512], i=ps_ap,
                      r=rsbc_ap[:, ib * 512:(ib + 1) * 512]:
                      e.tensor_tensor(out=o, in0=i, in1=r, op=ALU.mult), sdv)
                cnt = em.op("vector", lambda e, o=yst[m % 2][:, ib * 512:(ib + 1) * 512],
                            xr=xst[m % 2][:, ib * 512:(ib + 1) * 512]:
                            e.tensor_tensor(out=o, in0=o, in1=xr, op=ALU.add), sdv)
                wr_dv[(m, ib)] = cnt
                if ib == 3:
                    em.wait("sync", sdv, cnt)
                    em.op("sync", lambda e, d=dst[s, m], sr=yst[m % 2]: e.dma_start(out=d, in_=sr), sst, 16)
                    y_store[m] = sst.c
                return cnt

            emit_pv(XNT, symmetric, writer, rsbc)

        def ffn_phase(s, src, dst, xn_dv_c):
            XN = S0
            wb = [S3[:, 0:4096], S3[:, 4096:8192]]
            xstF = [S3f[:, 4096 + i * 1024: 5120 + i * 1024] for i in range(2)]
            ystF = [S3f[:, 6144 + i * 1024: 7168 + i * 1024] for i in range(2)]
            t1b = [S1f[:, j * 512:(j + 1) * 512] for j in range(4)]
            t2b = [S1f[:, 2048 + j * 512: 2560 + j * 512] for j in range(4)]
            tD = [S1f[:, j * 1024:(j + 1) * 1024] for j in range(2)]

            def gbuf(m):
                return (S1 if m < 16 else S2)[:, (m % 16) * 1024: (m % 16 + 1) * 1024]

            em.wait("tensor", sdv, xn_dv_c)
            for g in range(NG):
                fence(("tensor", "gpsimd"))
                evA = {}
                evA_ac = {}
                t_dv = {}
                for m in range(HC):
                    wd = wpanel(wb[m % 2][:, 0:2048], wA[m])
                    em.wait("tensor", swdge, wd)
                    if m >= 2:
                        em.wait("tensor", sdv, evA[m - 2])
                        em.wait("tensor", sac, evA_ac[m - 2])
                    ph = 1024 * (m % 2)
                    for th in range(2):
                        for ecu in range(EC):
                            em.op("tensor", lambda e, o=ps0[:, ph + th * 512: ph + (th + 1) * 512],
                                  l=wb[m % 2][:, ecu * 128:(ecu + 1) * 128],
                                  r=XN[:, ecu * S + g * GRP + th * 512: ecu * S + g * GRP + (th + 1) * 512],
                                  st=(ecu == 0), sp=(ecu == EC - 1):
                                  e.matmul(o, l, r, start=st, stop=sp))
                        for ecu in range(EC):
                            em.op("tensor", lambda e, o=ps1[:, ph + th * 512: ph + (th + 1) * 512],
                                  l=wb[m % 2][:, 1024 + ecu * 128: 1024 + (ecu + 1) * 128],
                                  r=XN[:, ecu * S + g * GRP + th * 512: ecu * S + g * GRP + (th + 1) * 512],
                                  st=(ecu == 0), sp=(ecu == EC - 1):
                                  e.matmul(o, l, r, start=st, stop=sp),
                                  swuse if (th == 1 and ecu == EC - 1) else None)
                    mm_done = swuse.c
                    for th in range(2):
                        j = (m * 2 + th) % 4
                        em.wait("scalar", swuse, mm_done)
                        em.wait("scalar", sdv, t_dv.get(("t", j, (m * 2 + th) - 4), 0))
                        em.op("scalar", lambda e, o=t1b[j], i=ps0[:, ph + th * 512: ph + (th + 1) * 512],
                              sc=s1c[:, m:m + 1]:
                              e.activation(out=o, in_=i, func=AF.Silu, scale=sc), sac)
                        t_ac = sac.c
                        em.wait("vector", swuse, mm_done)
                        em.op("vector", lambda e, o=t2b[j], i=ps1[:, ph + th * 512: ph + (th + 1) * 512],
                              sc=s2c[:, m:m + 1]:
                              e.tensor_scalar(out=o, in0=i, scalar1=sc, scalar2=None, op0=ALU.mult), sdv)
                        em.wait("vector", sac, t_ac)
                        cnt = em.op("vector", lambda e, o=SB[:, m * 1024 + th * 512: m * 1024 + (th + 1) * 512],
                                    a=t1b[j], bq=t2b[j]:
                                    e.tensor_tensor(out=o, in0=a, in1=bq, op=ALU.mult), sdv)
                        t_dv[("t", j, m * 2 + th)] = cnt
                    evA[m] = sdv.c
                    evA_ac[m] = sac.c

                def stage_dense(w_src, scol, nout, src_fn, dst_fn, epilogue):
                    fence(("tensor",))
                    evX = {}
                    for m in range(nout):
                        wd = wpanel(wb[m % 2], w_src[m])
                        em.wait("tensor", swdge, wd)
                        if m >= 2:
                            em.wait("tensor", sac, evX.get(("ac", m - 2), 0))
                            em.wait("tensor", sdv, evX.get(("dv", m - 2), 0))
                        ph = 1024 * (m % 2)
                        for k in range(HC):
                            for nb2 in range(2):
                                em.op("tensor", lambda e, o=ps0[:, ph + nb2 * 512: ph + (nb2 + 1) * 512],
                                      l=wb[m % 2][:, k * 128:(k + 1) * 128],
                                      r=src_fn(k)[:, nb2 * 512:(nb2 + 1) * 512],
                                      st=(k == 0), sp=(k == HC - 1):
                                      e.matmul(o, l, r, start=st, stop=sp),
                                      swuse if (k == HC - 1 and nb2 == 1) else None)
                        mm_done = swuse.c
                        epilogue(m, ph, mm_done, evX)
                    return evX

                def gelu_epi(scol):
                    def epi(m, ph, mm_done, evX):
                        em.wait("scalar", swuse, mm_done)
                        evX[("ac", m)] = em.op("scalar", lambda e, o=gbuf(m) if scol is h1c else SB[:, m * 1024:(m + 1) * 1024],
                                               i=ps0[:, ph: ph + 1024], sc=scol[:, m:m + 1]:
                                               e.activation(out=o, in_=i, func=AF.Gelu, scale=sc), sac)
                        evX[("dv", m)] = sdv.c
                    return epi

                def d_epi(m, ph, mm_done, evD):
                    if m >= 2:
                        em.wait("sync", sdv, evD.get(("dv", m - 2), 0))
                    em.op("sync", lambda e, d=xstF[m % 2], sr=src[s, m, :, g * GRP:(g + 1) * GRP]:
                          e.dma_start(out=d, in_=sr), sx, 16)
                    x_ld = sx.c
                    em.wait("scalar", swuse, mm_done)
                    em.wait("scalar", sdv, evD.get(("tD", m - 2), 0))
                    t_ac = em.op("scalar", lambda e, o=tD[m % 2], i=ps0[:, ph: ph + 1024], sc=w3c[:, m:m + 1]:
                                 e.activation(out=o, in_=i, func=AF.Identity, scale=sc), sac)
                    em.wait("vector", sac, t_ac)
                    em.wait("vector", sx, x_ld)
                    em.wait("vector", sst, evD.get(("st", m - 2), 0))
                    cnt = em.op("vector", lambda e, o=ystF[m % 2], a=tD[m % 2], xr=xstF[m % 2]:
                                e.tensor_tensor(out=o, in0=a, in1=xr, op=ALU.add), sdv)
                    evD[("dv", m)] = cnt
                    evD[("tD", m)] = cnt
                    em.wait("sync", sdv, cnt)
                    em.op("sync", lambda e, d=dst[s, m, :, g * GRP:(g + 1) * GRP], sr=ystF[m % 2]:
                          e.dma_start(out=d, in_=sr), sst, 16)
                    evD[("st", m)] = sst.c
                    evD[("ac", m)] = t_ac

                stage_dense(wH1, h1c, HC, lambda k: SB[:, k * 1024:(k + 1) * 1024], None, gelu_epi(h1c))
                stage_dense(wH2, h2c, HC, lambda k: gbuf(k), None, gelu_epi(h2c))
                stage_dense(wW3, w3c, EC, lambda k: SB[:, k * 1024:(k + 1) * 1024], None, d_epi)

        def attn_b_phase(s, src, dst, xn_dv_c):
            wpj = [SB[:, i * 1024:(i + 1) * 1024] for i in range(2)]
            rsbc = S3f[:, 1024:3072]
            xst = [S3f[:, 3072 + i * 2048: 5120 + i * 2048] for i in range(2)]
            yst = [S3f[:, 7168 + i * 512: 7680 + i * 512] for i in range(2)]
            wbp = [S3[:, 0:1024], S3[:, 1024:2048]]

            def proj(dst_slot, rhs_slot, w_idx, wbuf_pair):
                fence(("tensor", "gpsimd"))
                evp = {}
                for m in range(EC):
                    wd = wpanel(wbuf_pair[m % 2], wQK[w_idx, m])
                    em.wait("tensor", swdge, wd)
                    if m >= 2:
                        em.wait("tensor", sdv, evp.get(("dv", m - 2), 0))
                        em.wait("tensor", sac, evp.get(("ac", m - 2), 0))
                    psP = ps0 if m % 2 == 0 else ps1
                    for k in range(EC):
                        for nb in range(4):
                            em.op("tensor", lambda e, o=psP[:, nb * 512:(nb + 1) * 512],
                                  l=wbuf_pair[m % 2][:, k * 128:(k + 1) * 128],
                                  r=rhs_slot[:, k * S + nb * 512: k * S + (nb + 1) * 512],
                                  st=(k == 0), sp=(k == EC - 1):
                                  e.matmul(o, l, r, start=st, stop=sp),
                                  swuse if (k == EC - 1 and nb == 3) else None)
                    mm_done = swuse.c
                    if m % 2 == 0:
                        em.wait("vector", swuse, mm_done)
                        evp[("dv", m)] = em.op("vector", lambda e, o=dst_slot[:, m * S:(m + 1) * S], i=psP[:, :]:
                                               e.tensor_copy(out=o, in_=i), sdv)
                        evp[("ac", m)] = sac.c
                    else:
                        em.wait("scalar", swuse, mm_done)
                        evp[("ac", m)] = em.op("scalar", lambda e, o=dst_slot[:, m * S:(m + 1) * S], i=psP[:, :]:
                                               e.copy(out=o, in_=i), sac)
                        evp[("dv", m)] = sdv.c
                return evp

            em.wait("gpsimd", sdv, xn_dv_c)
            dump_slot("xn", S0[:, :], s)
            proj(S1, S0, 0, wpj)
            dump_slot("k", S1[:, :], s)
            proj(S2, S0, 1, wpj)
            dump_slot("q", S2[:, :], s)
            proj(S3, S0, 2, wpj)
            dump_slot("v", S3[:, :], s)
            proj(S0, S1, 3, wpj)
            dump_slot("ak", S0[:, :], s)
            fence(("tensor",))
            trv_dv = {}
            for jt in range(NQT):
                bb = jt % 2
                if jt >= 2:
                    em.wait("tensor", sdv, trv_dv[jt - 2])
                tp_pe = None
                for vc in range(EC):
                    tp_pe = em.op("tensor", lambda e, o=ps1b[:, bb * 1024 + vc * 128: bb * 1024 + (vc + 1) * 128],
                                  i=S3[:, vc * S + jt * 128: vc * S + (jt + 1) * 128]:
                                  e.transpose(o, i, identb[:, :]), spe if vc == EC - 1 else None)
                em.wait("vector", spe, tp_pe)
                trv_dv[jt] = em.op("vector", lambda e, o=S1[:, jt * 1024:(jt + 1) * 1024],
                                   i=ps1b[:, bb * 1024: bb * 1024 + 1024]:
                                   e.tensor_copy(out=o, in_=i), sdv)
            ev1 = {}
            for qt in range(NQT):
                psQ = ps0 if qt % 2 == 0 else ps1
                if qt == 1:
                    em.wait("tensor", sdv, trv_dv[NQT - 1])
                if qt >= 2:
                    em.wait("tensor", sac, ev1[qt - 2])
                e_pe = None
                for k in range(EC):
                    for nb in range(4):
                        e_pe = em.op("tensor", lambda e, o=psQ[:, nb * 512:(nb + 1) * 512],
                                     l=S2[:, k * S + qt * 128: k * S + qt * 128 + 128],
                                     r=S0[:, k * S + nb * 512: k * S + (nb + 1) * 512],
                                     st=(k == 0), sp=(k == EC - 1):
                                     e.matmul(o, l, r, start=st, stop=sp),
                                     spe if (k == EC - 1 and nb == 3) else None)
                em.wait("vector", spe, e_pe)
                nm_dv = em.op("vector", lambda e, o=nmcol[:, qt % 4: qt % 4 + 1], i=psQ[:, :]:
                              e.reduce_max(out=o, in_=i, axis=AX.X, negate=True), sdv)
                em.wait("scalar", sdv, nm_dv)
                ev1[qt] = em.op("scalar", lambda e, o=SB[:, qt * S:(qt + 1) * S], i=psQ[:, :],
                                b_=nmcol[:, qt % 4: qt % 4 + 1]:
                                e.activation(out=o, in_=i, func=AF.Exp, bias=b_), sac)
            em.wait("tensor", sac, sac.c)

            Z = S2

            def zwriter(m, ib, ps_ap, pe_cnt, rsbc_ap):
                em.wait("vector", spe, pe_cnt)
                return em.op("vector", lambda e, o=Z[:, m * S + ib * 512: m * S + (ib + 1) * 512], i=ps_ap:
                             e.tensor_copy(out=o, in_=i), sdv)

            emit_pv(S1, False, zwriter, rsbc)
            z_done = sdv.c
            dump_slot("Z", Z[:, :], s)
            dump_slot("PU", SB[:, :], s)
            dump_slot("rsbc", rsbc, s)
            dump_slot("vT", S1[:, :], s)
            fence(("tensor", "gpsimd", "sync"))
            em.wait("tensor", sdv, z_done)
            evw = {}
            stnb = {}
            for m in range(EC):
                wd = wpanel(wbp[m % 2], wQK[4, m])
                em.wait("tensor", swdge, wd)
                if m >= 2:
                    em.wait("tensor", sdv, evw.get(m - 2, 0))
                psP = ps0 if m % 2 == 0 else ps1
                if m >= 2:
                    em.wait("sync", sdv, evw.get(m - 2, 0))
                em.op("sync", lambda e, d=xst[m % 2], sr=src[s, m]: e.dma_start(out=d, in_=sr), sx, 16)
                x_ld = sx.c
                for k in range(EC):
                    for nb in range(4):
                        em.op("tensor", lambda e, o=psP[:, nb * 512:(nb + 1) * 512],
                              l=wbp[m % 2][:, k * 128:(k + 1) * 128],
                              r=Z[:, k * S + nb * 512: k * S + (nb + 1) * 512],
                              st=(k == 0), sp=(k == EC - 1):
                              e.matmul(o, l, r, start=st, stop=sp),
                              swuse if (k == EC - 1 and nb == 3) else None)
                mm_done = swuse.c
                em.wait("vector", swuse, mm_done)
                em.wait("vector", sx, x_ld)
                for nb in range(4):
                    yb = yst[nb % 2]
                    em.wait("vector", sst, stnb.get(m * 4 + nb - 2, 0))
                    em.op("vector", lambda e, o=yb, i=psP[:, nb * 512:(nb + 1) * 512],
                          r=rsbc[:, nb * 512:(nb + 1) * 512]:
                          e.tensor_tensor(out=o, in0=i, in1=r, op=ALU.mult), sdv)
                    cnt = em.op("vector", lambda e, o=yb, xr=xst[m % 2][:, nb * 512:(nb + 1) * 512]:
                                e.tensor_tensor(out=o, in0=o, in1=xr, op=ALU.add), sdv)
                    em.wait("sync", sdv, cnt)
                    em.op("sync", lambda e, d=dst[s, m, :, nb * 512:(nb + 1) * 512], sr=yb:
                          e.dma_start(out=d, in_=sr), sst, 16)
                    stnb[m * 4 + nb] = sst.c
                evw[m] = sdv.c

        em.op("sync", lambda e: e.dma_start(out=identf[:], in_=idf[:, :]), sx, 16)
        em.op("sync", lambda e: e.dma_start(out=svst[:, :], in_=svec4[:, :]), sx, 16)
        em.op("sync", lambda e: e.dma_start(out=svst3[:, :], in_=svec3[:, :]), sx, 16)
        em.op("sync", lambda e: e.dma_start(out=g11[:, :], in_=gam[:, :]), sx, 16)
        prolog_sx = sx.c
        em.wait("vector", sx, prolog_sx)
        em.op("vector", lambda e: e.tensor_copy(out=identb[:], in_=identf[:]), sdv)
        em.op("vector", lambda e: e.memset(ones128b[:], 1.0), sdv)
        em.op("vector", lambda e: e.memset(onesrow[0:1, 0:128], 1.0), sdv)
        em.op("vector", lambda e: e.memset(onesrow[0:1, 128:256], 1.0 / E), sdv)
        em.op("vector", lambda e: e.memset(onesrow[0:1, 256:384], float(E)), sdv)
        em.op("vector", lambda e: e.memset(onesrow[0:1, 384:512], -1.0), sdv)
        prolog_dv = em.op("vector", lambda e: e.memset(epsc[0:1, 0:1], EPS), sdv)
        em.wait("tensor", sx, prolog_sx)
        em.wait("tensor", sdv, prolog_dv)
        prolog_pe = None
        for i in range(4):
            prolog_pe = em.op("tensor", lambda e, o=ps0[:, i * 512: i * 512 + 32],
                              src_=svst[0:32, i * 128:(i + 1) * 128]:
                              e.transpose(o, src_, identf[0:32, 0:32]), spe)
        prolog_pe = em.op("tensor", lambda e: e.transpose(ps1[:, 0:8], svst3[0:8, :], identf[0:8, 0:8]), spe)
        prolog_pe = em.op("tensor", lambda e: e.matmul(ps1[:, 512:513], ONE_NEG, g11[0:1, 0:1],
                                                       start=True, stop=True), spe)
        em.wait("vector", spe, prolog_pe)
        for i, dstt in enumerate([s1c, s2c, h1c, h2c]):
            em.op("vector", lambda e, o=dstt[:, :], i_=ps0[:, i * 512: i * 512 + 32]:
                  e.tensor_copy(out=o, in_=i_), sdv)
        em.op("vector", lambda e: e.tensor_copy(out=w3c[:, :], in_=ps1[:, 0:8]), sdv)
        prolog_done = em.op("vector", lambda e: e.tensor_copy(out=gneg[:, :], in_=ps1[:, 512:513]), sdv)
        em.wait("tensor", sdv, prolog_done)

        kinds = kinds if kinds is not None else ["A", "F", "C", "F"][:n_sub]
        n_sub = len(kinds)
        for s in range(SEQ):
            for sub, kind in enumerate(kinds):
                src = xin if sub == 0 else xres
                dst = y if sub == n_sub - 1 else xres
                xn_c, sq_c = ln_phase(s, src, want_sq=(kind == "A"))
                if kind == "A":
                    attn_a_phase(s, src, dst, xn_c, sq_c)
                elif kind == "F":
                    ffn_phase(s, src, dst, xn_c)
                else:
                    attn_b_phase(s, src, dst, xn_c)

        final_st = sst.c

        @block.tensor
        def _(eng):
            for f in em.ops["tensor"]:
                f(eng)

        @block.vector
        def _(eng):
            for f in em.ops["vector"]:
                f(eng)

        @block.scalar
        def _(eng):
            for f in em.ops["scalar"]:
                f(eng)

        @block.gpsimd
        def _(eng):
            for f in em.ops["gpsimd"]:
                f(eng)

        @block.sync
        def _(eng):
            for f in em.ops["sync"]:
                f(eng)
            eng.wait_ge(sst.h, final_st)

    return nc


def _tile_weight(w, out_chunks, in_chunks):
    O, I = w.shape
    a = w.reshape(out_chunks, 128, in_chunks, 128)
    return np.ascontiguousarray(a.transpose(0, 3, 2, 1).reshape(out_chunks, 128, I))


def _stage(inputs):
    f = np.float32
    x = np.asarray(inputs["L0_attn_a_x"], f)
    xfm = np.ascontiguousarray(x.reshape(B, S, EC, 128).transpose(0, 2, 3, 1))
    w1 = _tile_weight(np.asarray(inputs["L0_ffn_w1_w"], f), HC, EC)
    w2 = _tile_weight(np.asarray(inputs["L0_ffn_w2_w"], f), HC, EC)
    wA_ = np.ascontiguousarray(np.concatenate([w1, w2], axis=2))
    wH1_ = _tile_weight(np.asarray(inputs["L0_ffn_h1_w"], f), HC, HC)
    wH2_ = _tile_weight(np.asarray(inputs["L0_ffn_h2_w"], f), HC, HC)
    wW3_ = _tile_weight(np.asarray(inputs["L0_ffn_w3_w"], f), EC, HC)
    wQK_ = np.ascontiguousarray(np.stack([
        _tile_weight(np.asarray(inputs["L0_attn_b_Wk"], f), EC, EC),
        _tile_weight(np.asarray(inputs["L0_attn_b_Wq"], f), EC, EC),
        _tile_weight(np.asarray(inputs["L0_attn_b_Wv"], f), EC, EC),
        _tile_weight(np.asarray(inputs["L0_attn_b_Wa"], f), EC, EC),
        _tile_weight(np.asarray(inputs["L0_attn_b_Wb"], f), EC, EC),
    ]))
    svec4_ = np.ascontiguousarray(np.stack([
        np.asarray(inputs["L0_ffn_w1_s"], f).reshape(32, 128),
        np.asarray(inputs["L0_ffn_w2_s"], f).reshape(32, 128),
        np.asarray(inputs["L0_ffn_h1_s"], f).reshape(32, 128),
        np.asarray(inputs["L0_ffn_h2_s"], f).reshape(32, 128),
    ]).transpose(1, 0, 2).reshape(32, 512))
    svec3_ = np.ascontiguousarray(np.asarray(inputs["L0_ffn_w3_s"], f).reshape(8, 128))
    gamma = np.asarray(inputs["L0_attn_a_gamma"], f).reshape(1, 1)
    shared = dict(wA=wA_, wH1=wH1_, wH2=wH2_, wW3=wW3_, wQK=wQK_, svec4=svec4_,
                  svec3=svec3_, gam=gamma,
                  idf=np.eye(128, dtype=f))
    in_maps = []
    for c in range(NCORES):
        m = dict(shared)
        m["xin"] = np.ascontiguousarray(xfm[c * SEQ:(c + 1) * SEQ])
        in_maps.append(m)
    return in_maps, float(gamma[0, 0])


def _kernel_impl(inputs, n_sub=4, kinds=None, dump=(), want_raw=False):
    in_maps, gval = _stage(inputs)
    nc = build_program(n_sub=n_sub, neg_gamma=(gval < 0), kinds=kinds, dump=dump)
    res = run_bass_kernel_spmd(nc, in_maps, core_ids=list(range(NCORES)))
    yfm = np.concatenate([r["y"] for r in res.results], axis=0)
    out = np.ascontiguousarray(yfm.transpose(0, 3, 1, 2).reshape(B, S, E))
    if want_raw:
        return out, res.results[0]
    return out


def kernel(**inputs):
    return _kernel_impl(inputs, n_sub=4)
